# version 04
# baseline (speedup 1.0000x reference)
import numpy as np
import ml_dtypes
import concourse.bass as bass
import concourse.mybir as mybir
from concourse.bass_utils import run_bass_kernel_spmd

F32 = mybir.dt.float32
BF16 = mybir.dt.bfloat16
AF = mybir.ActivationFunctionType
OP = mybir.AluOpType

NCORES = 8
T = 2048
D = 1024
HID = 4096
KD = D // 128
KH = HID // 128
NG = 4
TG = T // NG
NT = T // 128
EPS = 1e-5

ENGINES = ("tensor", "vector", "scalar", "gpsimd", "sync")


class Sched:

    def __init__(self):
        self.q = {e: [] for e in ENGINES}
        self.n = {}
        self.waited = {e: {} for e in ENGINES}

    def emit(self, eng, fn, waits=(), inc=None, inc_amt=1):
        ws = []
        for s, v in waits:
            if v <= 0:
                continue
            if self.waited[eng].get(s, 0) >= v:
                continue
            self.waited[eng][s] = v
            ws.append((s, v))
        self.q[eng].append((fn, ws, inc, inc_amt))
        if inc is not None:
            self.n[inc] = self.n.get(inc, 0) + inc_amt
            return self.n[inc]
        return None

    def count(self, sem):
        return self.n.get(sem, 0)

    def replay(self, engname, eng, sems):
        for fn, ws, inc, amt in self.q[engname]:
            for s, v in ws:
                eng.wait_ge(sems[s], v)
            inst = fn(eng)
            if inc is not None:
                inst.then_inc(sems[inc], amt)


DEBUG = False


def build_program():
    nc = bass.Bass("TRN2", target_bir_lowering=False, debug=DEBUG)

    xT = nc.dram_tensor("xT", [D, T], F32, kind="ExternalInput").ap()
    wq = nc.dram_tensor("wq", [4 * 128, 2048], F32, kind="ExternalInput").ap()
    wk = nc.dram_tensor("wk", [4 * 128, 2048], F32, kind="ExternalInput").ap()
    wv = nc.dram_tensor("wv", [4 * 128, 2048], F32, kind="ExternalInput").ap()
    wg = nc.dram_tensor("wg", [4 * 128, 2048], F32, kind="ExternalInput").ap()
    wo = nc.dram_tensor("wo", [4 * 128, 2048], F32, kind="ExternalInput").ap()
    bq = nc.dram_tensor("bq", [D], F32, kind="ExternalInput").ap()
    bk = nc.dram_tensor("bk", [D], F32, kind="ExternalInput").ap()
    bv = nc.dram_tensor("bv", [D], F32, kind="ExternalInput").ap()
    bg = nc.dram_tensor("bg", [D], F32, kind="ExternalInput").ap()
    bo = nc.dram_tensor("bo", [D], F32, kind="ExternalInput").ap()
    w1t = nc.dram_tensor("w1t", [16 * 128, 2048], F32, kind="ExternalInput").ap()
    w2t = nc.dram_tensor("w2t", [16 * 128, 2048], F32, kind="ExternalInput").ap()
    h1t = nc.dram_tensor("h1t", [64 * 128, 2048], F32, kind="ExternalInput").ap()
    h2t = nc.dram_tensor("h2t", [64 * 128, 2048], F32, kind="ExternalInput").ap()
    w3t = nc.dram_tensor("w3t", [16 * 128, 2048], F32, kind="ExternalInput").ap()
    s1 = nc.dram_tensor("s1", [HID], F32, kind="ExternalInput").ap()
    s2 = nc.dram_tensor("s2", [HID], F32, kind="ExternalInput").ap()
    sh1 = nc.dram_tensor("sh1", [HID], F32, kind="ExternalInput").ap()
    sh2 = nc.dram_tensor("sh2", [HID], F32, kind="ExternalInput").ap()
    s3 = nc.dram_tensor("s3", [D], F32, kind="ExternalInput").ap()
    yT = nc.dram_tensor("yT", [D, T], F32, kind="ExternalOutput").ap()
    cc_in = nc.dram_tensor("cc_in", [128, 512], F32)
    cc_out = nc.dram_tensor("cc_out", [128, 512], F32)

    sch = Sched()
    import contextlib
    with contextlib.ExitStack() as ctx:
        def sb(name, shape, dt):
            return ctx.enter_context(nc.sbuf_tensor(name, shape, dt))

        def psum(name, shape, dt=F32):
            return ctx.enter_context(nc.psum_tensor(name, shape, dt))

        xn = sb("xn", [128, KD * T], BF16)
        xf = sb("xf", [128, 2 * 512], F32)
        xbf = sb("xbf", [128, 2 * 512], BF16)
        x2 = sb("x2", [128, 2 * 512], BF16)
        pool = [sb(f"wp{i}", [128, 2048], BF16) for i in range(8)]
        q_g = sb("q_g", [128, KD * TG], BF16)
        gt_g = sb("gt_g", [128, KD * TG], BF16)
        hb = sb("hb", [128, KH * TG], BF16)
        gb = sb("gb", [128, KH * TG], BF16)
        k_tm = sb("k_tm", [128, 2 * D], BF16)
        v_tm = sb("v_tm", [128, 2 * D], BF16)
        kv_sb = sb("kv_sb", [128, 512], F32)
        kv_rs = sb("kv_rs", [128, 512], F32)
        kv_bf = sb("kv_bf", [128, 512], BF16)
        tA = sb("tA", [128, 512], F32)
        tB = sb("tB", [128, 512], F32)
        tC = sb("tC", [128, 512], F32)
        tD = sb("tD", [128, 512], F32)
        sil = sb("sil", [128, 512], BF16)
        bkb = sb("bkb", [128, D], BF16)
        bvb = sb("bvb", [128, D], BF16)
        bq_c = sb("bq_c", [128, KD], F32)
        bg_c = sb("bg_c", [128, KD], F32)
        bo_c = sb("bo_c", [128, KD], F32)
        s1_c = sb("s1_c", [128, KH], F32)
        s2_c = sb("s2_c", [128, KH], F32)
        sh1_c = sb("sh1_c", [128, KH], F32)
        sh2_c = sb("sh2_c", [128, KH], F32)
        s3_c = sb("s3_c", [128, KD], F32)
        ones_c = sb("ones_c", [128, 1], BF16)
        ones_r = sb("ones_r", [1, 128], BF16)
        brow = sb("brow", [1, 2 * D], F32)
        bbf = sb("bbf", [1, 512], BF16)
        mrow = sb("mrow", [1, 512], F32)
        vrow = sb("vrow", [1, 512], F32)
        irow = sb("irow", [1, 512], F32)
        arow = sb("arow", [1, 512], BF16)
        brow2 = sb("brow2", [1, 512], F32)
        lnbf = sb("lnbf", [1, 512], BF16)
        xr = sb("xr", [128, 2 * 512], F32)
        out_sb = sb("out_sb", [128, 2 * 512], F32)

        pb = [psum(f"pb{i}", [128, 512]) for i in range(8)]

        sems = {}
        semnames = (["sPE", "sACT", "sDVE", "sXN", "sCC", "sSET",
                     "sXF0", "sXF1", "sXR0", "sXR1", "sKVB", "sKVR",
                     "sST0", "sST1"] + [f"sW{i}" for i in range(8)])
        for s in semnames:
            sems[s] = ctx.enter_context(nc.semaphore(s))

        def dma(eng, out, in_, waits=(), inc=None, slow=False):
            def fn(e, out=out, in_=in_, slow=slow):
                return e.dma_start(out=out, in_=in_,
                                   allow_slow_non_contiguous=slow)
            return sch.emit(eng, fn, waits, inc, 16)

        def mm(out, lhsT, rhs, start, stop, waits=(), inc=None, tp=None):
            def fn(e, out=out, lhsT=lhsT, rhs=rhs, start=start, stop=stop,
                   tp=tp):
                return e.matmul(out, lhsT, rhs, start=start, stop=stop,
                                tile_position=tp)
            return sch.emit("tensor", fn, waits, inc, 1)

        def act(out, in_, func, bias=None, scale=None, waits=(), inc="sACT"):
            waits = list(waits) + [("sACT", sch.count("sACT"))]
            def fn(e, out=out, in_=in_, func=func, bias=bias, scale=scale):
                kw = {}
                if bias is not None:
                    kw["bias"] = bias
                if scale is not None:
                    kw["scale"] = scale
                return e.activation(out, in_, func, **kw)
            return sch.emit("scalar", fn, waits, inc, 1)

        def dve(method, *args, waits=(), inc="sDVE"):
            waits = (list(waits) + [("sDVE", sch.count("sDVE")),
                                    ("sXN", sch.count("sXN"))])
            def fn(e, method=method, args=args):
                return getattr(e, method)(*args)
            return sch.emit("vector", fn, waits, inc, 1)

        def half_src(w, hh):
            return w[hh * 128:(hh + 1) * 128, :]

        def sq_lhsT(slots, k, c):
            sl = slots[(c // 4) * 2 + (k // 4)]
            off = (k % 4) * 512 + (c % 4) * 128
            return pool[sl][:, off:off + 128]

        def sq_rhs(slots, k, hb_):
            sl = slots[hb_ * 2 + (k // 4)]
            return pool[sl][:, (k % 4) * 512:(k % 4) * 512 + 512]

        def xn_s(k, t0, n):
            return xn[:, k * T + t0: k * T + t0 + n]

        slot_free = [None] * 8
        slot_loads = [0] * 8

        def load_half(slot, src, waits=()):
            w = list(waits)
            if slot_free[slot] is not None:
                w.append(slot_free[slot])
            dma("gpsimd", pool[slot][:, :], src, waits=w, inc=f"sW{slot}")
            slot_loads[slot] += 1

        def slots_wait(slots):
            return [(f"sW{s}", 16 * slot_loads[s]) for s in sorted(set(slots))]

        rr = [0]

        def next_slots(n):
            out = [(rr[0] + i) % 8 for i in range(n)]
            rr[0] = (rr[0] + n) % 8
            return out

        eps_c = sb("eps_c", [128, 1], F32)
        sch.emit("vector", lambda e: e.memset(ones_c[:, :], 1.0), (), "sDVE", 1)
        sch.emit("vector", lambda e: e.memset(ones_r[:, :], 1.0), (), "sDVE", 1)
        sch.emit("vector", lambda e: e.memset(eps_c[:, :], EPS), (), "sDVE", 1)
        nc.const_aps.register(F32, EPS, eps_c[:, :])
        ones_ready = sch.count("sDVE")

        for dst, src in ((bq_c, bq), (bg_c, bg), (bo_c, bo), (s1_c, s1),
                         (s2_c, s2), (sh1_c, sh1), (sh2_c, sh2), (s3_c, s3)):
            dma("sync", dst[:, :], src.rearrange("(c p) -> p c", p=128),
                inc="sSET", slow=True)
        dma("sync", brow[0:1, 0:D], bk.rearrange("(a d) -> a d", a=1),
            inc="sSET")
        dma("sync", brow[0:1, D:2 * D], bv.rearrange("(a d) -> a d", a=1),
            inc="sSET")
        setup_x = ("sSET", sch.count("sSET"))
        dsts = [bkb[:, 0:512], bkb[:, 512:1024], bvb[:, 0:512],
                bvb[:, 512:1024]]
        prev_copy = 0
        evac_dve = [0, 0]
        for i in range(4):
            bank = pb[6 + i % 2]
            a_c = act(bbf[0:1, :], brow[0:1, i * 512:(i + 1) * 512], AF.Copy,
                      waits=[setup_x, ("sPE", prev_copy)])
            pe_v = mm(bank[:, :], ones_r[:, :], bbf[0:1, :], True, True,
                      waits=[("sACT", a_c), ("sDVE", ones_ready),
                             ("sDVE", evac_dve[i % 2])], inc="sPE")
            prev_copy = pe_v
            evac_dve[i % 2] = dve("tensor_copy", dsts[i], bank[:, :],
                                  waits=[("sPE", pe_v)])
        bcast_done_dve = sch.count("sDVE")
        bcast_pe = sch.count("sPE")

        wk_slots = next_slots(4)
        for hh in range(4):
            load_half(wk_slots[hh], half_src(wk, hh))
        wv_slots = next_slots(4)
        for hh in range(4):
            load_half(wv_slots[hh], half_src(wv, hh))
        wkv_loaded = slots_wait(wk_slots + wv_slots)

        xf_read = [("sDVE", 0), ("sDVE", 0)]

        def load_x_chunk(k, g, bi):
            dma("sync", xf[:, bi * 512:(bi + 1) * 512],
                xT[k * 128:(k + 1) * 128, g * TG:(g + 1) * TG],
                waits=[xf_read[bi]], inc=f"sXF{bi}")
            return (f"sXF{bi}", sch.count(f"sXF{bi}"))

        prev_stats_pe = 0
        prev_d1_dve = 0
        prev_norm_dve = 0
        prev_a1 = 0
        for g in range(NG):
            prev_xbf_pe = prev_stats_pe
            for k in range(KD):
                bi = k % 2
                xv = load_x_chunk(k, g, bi)
                cpy = act(xbf[:, bi * 512:(bi + 1) * 512],
                          xf[:, bi * 512:(bi + 1) * 512], AF.Copy,
                          waits=[xv, ("sPE", prev_xbf_pe)])
                xf_read[bi] = ("sACT", cpy)
                sq = act(x2[:, bi * 512:(bi + 1) * 512],
                         xbf[:, bi * 512:(bi + 1) * 512], AF.Square)
                mm(pb[0][0:1, :], ones_c[:, :],
                   xbf[:, bi * 512:(bi + 1) * 512], k == 0, k == KD - 1,
                   waits=[("sACT", cpy), ("sDVE", prev_d1_dve),
                          ("sDVE", ones_ready)])
                prev_xbf_pe = mm(pb[1][0:1, :], ones_c[:, :],
                                 x2[:, bi * 512:(bi + 1) * 512],
                                 k == 0, k == KD - 1,
                                 waits=[("sACT", sq)], inc="sPE")
            stats_pe = sch.count("sPE")
            prev_stats_pe = stats_pe
            d1 = dve("tensor_scalar", mrow[0:1, :], pb[0][0:1, :], 1.0 / D,
                     None, OP.mult, waits=[("sPE", stats_pe)])
            d2 = dve("tensor_tensor", vrow[0:1, :], mrow[0:1, :],
                     mrow[0:1, :], OP.mult, waits=[("sACT", prev_a1)])
            d3 = dve("scalar_tensor_tensor", vrow[0:1, :], pb[1][0:1, :],
                     1.0 / D, vrow[0:1, :], OP.mult, OP.subtract)
            prev_d1_dve = d3
            a1 = act(irow[0:1, :], vrow[0:1, :], AF.Sqrt, bias=EPS,
                     waits=[("sDVE", d3)])
            d3b = dve("reciprocal", vrow[0:1, :], irow[0:1, :],
                      waits=[("sACT", a1)])
            a2 = act(arow[0:1, :], vrow[0:1, :], AF.Copy,
                     waits=[("sDVE", d3b)])
            prev_a1 = a2
            d4 = dve("tensor_tensor", brow2[0:1, :], mrow[0:1, :],
                     vrow[0:1, :], OP.mult)
            d5 = dve("tensor_scalar", lnbf[0:1, :], brow2[0:1, :], -1.0,
                     None, OP.mult)
            mm(pb[2][:, :], ones_r[:, :], arow[0:1, :], True, True,
               waits=[("sACT", a2), ("sDVE", prev_norm_dve),
                      ("sXN", 8 * g)])
            ab_pe = mm(pb[3][:, :], ones_r[:, :], lnbf[0:1, :], True, True,
                       waits=[("sDVE", d5)], inc="sPE")
            for k in range(KD):
                bi = k % 2
                xv = load_x_chunk(k, g, bi)
                d6 = dve("tensor_tensor", tA[:, :],
                         xf[:, bi * 512:(bi + 1) * 512], pb[2][:, :],
                         OP.mult, waits=[xv, ("sPE", ab_pe)])
                xf_read[bi] = ("sDVE", d6)
                dve("tensor_tensor", xn_s(k, g * TG, TG), tA[:, :],
                    pb[3][:, :], OP.add, inc="sXN")
            prev_norm_dve = sch.count("sDVE")

        ph1a_dve_end = sch.count("sDVE")

        proj_pe = {}
        epi_dve = {}
        kv_pe = {}

        def emit_proj(t):
            g = t // 4
            waits0 = [("sXN", 8 * (g + 1))] + wkv_loaded
            if t == 0:
                waits0 += [("sXN", 8 * NG), ("sDVE", ph1a_dve_end)]
            if t >= 1:
                waits0.append(("sDVE", bank_free_dve[t - 1]))
            for k in range(KD):
                lhsT = xn[:, k * T + t * 128: k * T + t * 128 + 128]
                w = waits0 if k == 0 else ()
                mm(pb[0][:, :], lhsT, sq_rhs(wk_slots, k, 0), k == 0,
                   k == KD - 1, waits=w)
                mm(pb[1][:, :], lhsT, sq_rhs(wk_slots, k, 1), k == 0,
                   k == KD - 1)
                mm(pb[2][:, :], lhsT, sq_rhs(wv_slots, k, 0), k == 0,
                   k == KD - 1)
                mm(pb[3][:, :], lhsT, sq_rhs(wv_slots, k, 1), k == 0,
                   k == KD - 1, inc="sPE" if k == KD - 1 else None)
            proj_pe[t] = sch.count("sPE")

        bank_free_dve = {}

        def emit_epilogue(t):
            bi = t % 2
            w = [("sPE", proj_pe[t])]
            wk_ = list(w)
            if t >= 2:
                wk_.append(("sPE", kv_pe[t - 2]))
            dve("tensor_tensor", tA[:, :], pb[0][:, :], bkb[:, 0:512],
                OP.add, waits=w)
            dve("tensor_tensor", tD[:, :], pb[1][:, :], bkb[:, 512:1024],
                OP.add)
            dve("tensor_tensor", v_tm[:, bi * D: bi * D + 512],
                pb[2][:, :], bvb[:, 0:512], OP.add, waits=wk_)
            dve("tensor_tensor", v_tm[:, bi * D + 512: bi * D + 1024],
                pb[3][:, :], bvb[:, 512:1024], OP.add)
            bank_free_dve[t] = sch.count("sDVE")
            for hb_, zb in ((0, tA), (1, tD)):
                d_zm = dve("tensor_scalar", tB[:, :], zb[:, :], 0.0, None,
                           OP.min)
                a_e = act(tC[:, :], tB[:, :], AF.Exp, waits=[("sDVE", d_zm)])
                dve("scalar_tensor_tensor",
                    k_tm[:, bi * D + hb_ * 512: bi * D + hb_ * 512 + 512],
                    zb[:, :], 0.0, tC[:, :], OP.max, OP.add,
                    waits=[("sACT", a_e)])
            epi_dve[t] = sch.count("sDVE")

        def emit_kv(t):
            bi = t % 2
            for p in range(8):
                bank = pb[4 + p // 4]
                sl = bank[:, (p % 4) * 128:(p % 4) * 128 + 128]
                start = (t == 0) and (p % 4 == 0)
                stop = (t == NT - 1) and (p % 4 == 3)
                w = [("sDVE", epi_dve[t])] if p == 0 else ()
                mm(sl, k_tm[:, bi * D + p * 128: bi * D + p * 128 + 128],
                   v_tm[:, bi * D + p * 128: bi * D + p * 128 + 128],
                   start, stop, waits=w, inc="sPE" if p == 7 else None)
            kv_pe[t] = sch.count("sPE")

        for t in range(NT):
            emit_proj(t)
            emit_epilogue(t)
            if t >= 1:
                emit_kv(t - 1)
        emit_kv(NT - 1)
        kv_done_pe = sch.count("sPE")

        for p in range(8):
            bank = pb[4 + p // 4]
            dve("tensor_copy", kv_sb[0:64, p * 64:(p + 1) * 64],
                bank[0:64, (p % 4) * 128:(p % 4) * 128 + 64],
                waits=[("sPE", kv_done_pe)])
            dve("tensor_copy", kv_sb[64:128, p * 64:(p + 1) * 64],
                bank[64:128, (p % 4) * 128 + 64:(p % 4) * 128 + 128])
        kv_evac_dve = sch.count("sDVE")
        dma("sync", cc_in[:, :], kv_sb[:, :], waits=[("sDVE", kv_evac_dve)],
            inc="sKVB")

        def do_cc(e):
            return e.collective_compute(
                "AllReduce", OP.add,
                replica_groups=[[0, 1, 2, 3], [4, 5, 6, 7]],
                ins=[cc_in.ap().opt()], outs=[cc_out.ap().opt()])
        sch.emit("gpsimd", do_cc, [("sKVB", 16)], "sCC", 1)
        dma("sync", kv_rs[:, :], cc_out[:, :], waits=[("sCC", 1)], inc="sKVR")
        kv_cast = act(kv_bf[:, :], kv_rs[:, :], AF.Copy,
                      waits=[("sKVR", 16)])

        bank_free = {i: ("sPE", 0) for i in range(8)}
        bank_free[4] = ("sDVE", kv_evac_dve)
        bank_free[5] = ("sDVE", kv_evac_dve)
        bank_free[6] = ("sDVE", bcast_done_dve)
        bank_free[7] = ("sDVE", bcast_done_dve)
        for i in range(8):
            slot_free[i] = ("sPE", proj_pe[NT - 1])

        store_n = [0]
        prev_out_read = {0: ("sST0", 0), 1: ("sST1", 0)}
        prev_xr_read = {0: ("sDVE", 0), 1: ("sDVE", 0)}
        tmp_free = ("sDVE", ph1a_dve_end)
        prev_sil_read = ("sDVE", 0)
        prev_ev_act = 0

        for g in range(NG):
            t0 = g * TG
            xn_ready = ("sXN", 8 * (g + 1))

            wg_slots = next_slots(4)
            for hh in range(4):
                load_half(wg_slots[hh], half_src(wg, hh))
            wg_w = slots_wait(wg_slots)
            wq_slots = next_slots(4)
            for hh in range(4):
                load_half(wq_slots[hh], half_src(wq, hh))
            wq_w = slots_wait(wq_slots)

            for c in range(KD):
                b = c % 2
                for k in range(KD):
                    w = [xn_ready, bank_free[b]] + wg_w if k == 0 \
                        else ()
                    mm(pb[b][:, :], sq_lhsT(wg_slots, k, c), xn_s(k, t0, TG),
                       k == 0, k == KD - 1, waits=w,
                       inc="sPE" if k == KD - 1 else None)
                pe_v = sch.count("sPE")
                ev = act(gt_g[:, c * TG:(c + 1) * TG], pb[b][:, :],
                         AF.Sigmoid, bias=bg_c[:, c:c + 1],
                         waits=[("sPE", pe_v)])
                bank_free[b] = ("sACT", ev)
            gate_act_end = sch.count("sACT")
            gate_pe_end = sch.count("sPE")
            for s in wg_slots:
                slot_free[s] = ("sPE", gate_pe_end)

            for c in range(KD):
                b = c % 2
                for k in range(KD):
                    w = [bank_free[b]] + wq_w if k == 0 else ()
                    mm(pb[b][:, :], sq_lhsT(wq_slots, k, c), xn_s(k, t0, TG),
                       k == 0, k == KD - 1, waits=w,
                       inc="sPE" if k == KD - 1 else None)
                pe_v = sch.count("sPE")
                d_zm = dve("tensor_scalar", tA[:, :], pb[b][:, :],
                           bq_c[:, c:c + 1], 0.0, OP.add, OP.min,
                           waits=[("sPE", pe_v), tmp_free])
                d_r = dve("tensor_scalar", tB[:, :], pb[b][:, :],
                          bq_c[:, c:c + 1], 0.0, OP.add, OP.max)
                bank_free[b] = ("sDVE", d_r)
                a_e = act(tC[:, :], tA[:, :], AF.Exp, waits=[("sDVE", d_zm)])
                d_q = dve("tensor_tensor", q_g[:, c * TG:(c + 1) * TG],
                          tB[:, :], tC[:, :], OP.add, waits=[("sACT", a_e)])
                tmp_free = ("sDVE", d_q)
            q_done_dve = sch.count("sDVE")
            for s in wq_slots:
                slot_free[s] = ("sPE", sch.count("sPE"))

            w12 = {}

            def load_w12(ob):
                sl = next_slots(4)
                for h2 in range(2):
                    load_half(sl[h2], half_src(w1t, ob * 2 + h2))
                for h2 in range(2):
                    load_half(sl[2 + h2], half_src(w2t, ob * 2 + h2))
                w12[ob] = (sl, slots_wait(sl))

            load_w12(0)
            load_w12(1)
            for o in range(KH):
                ob = o // 4
                sl, wv_ = w12[ob]
                b1 = (o % 2) * 2
                b2 = b1 + 1
                for k in range(KD):
                    w = ([bank_free[b1], bank_free[b2]] + wv_) \
                        if k == 0 else ()
                    off = (k % 4) * 512 + (o % 4) * 128
                    mm(pb[b1][:, :], pool[sl[k // 4]][:, off:off + 128],
                       xn_s(k, t0, TG), k == 0, k == KD - 1, waits=w)
                    mm(pb[b2][:, :], pool[sl[2 + k // 4]][:, off:off + 128],
                       xn_s(k, t0, TG), k == 0, k == KD - 1,
                       inc="sPE" if k == KD - 1 else None)
                pe_v = sch.count("sPE")
                a_s = act(sil[:, :], pb[b1][:, :], AF.Silu,
                          scale=s1_c[:, o:o + 1],
                          waits=[("sPE", pe_v), prev_sil_read])
                bank_free[b1] = ("sACT", a_s)
                d_h = dve("scalar_tensor_tensor", hb[:, o * TG:(o + 1) * TG],
                          pb[b2][:, :], s2_c[:, o:o + 1], sil[:, :],
                          OP.mult, OP.mult, waits=[("sACT", a_s)])
                bank_free[b2] = ("sDVE", d_h)
                prev_sil_read = ("sDVE", d_h)
                if o % 4 == 3:
                    for s in sl:
                        slot_free[s] = ("sPE", pe_v)
                    if ob + 2 < 8:
                        load_w12(ob + 2)
            y_dve_end = sch.count("sDVE")

            in_ready = [("sDVE", y_dve_end), None]
            for layer, (wt, sc, dstb, srcb) in enumerate(
                    ((h1t, sh1_c, gb, hb), (h2t, sh2_c, hb, gb))):
                for o2 in range(KH):
                    sl = next_slots(2)
                    for h in range(2):
                        load_half(sl[h], half_src(wt, o2 * 2 + h))
                    w_w = slots_wait(sl)
                    b = o2 % 4
                    for k2 in range(KH):
                        w = ()
                        if k2 == 0:
                            w = [bank_free[b], in_ready[layer]] + w_w
                        mm(pb[b][:, :],
                           pool[sl[k2 // 16]][:, (k2 % 16) * 128:
                                              (k2 % 16) * 128 + 128],
                           srcb[:, k2 * TG:(k2 + 1) * TG],
                           k2 == 0, k2 == KH - 1, waits=w,
                           inc="sPE" if k2 == KH - 1 else None)
                    pe_v = sch.count("sPE")
                    for s in sl:
                        slot_free[s] = ("sPE", pe_v)
                    ev = act(dstb[:, o2 * TG:(o2 + 1) * TG], pb[b][:, :],
                             AF.Gelu, scale=sc[:, o2:o2 + 1],
                             waits=[("sPE", pe_v)])
                    bank_free[b] = ("sACT", ev)
                if layer == 0:
                    in_ready[1] = ("sACT", sch.count("sACT"))
            g2_act_end = sch.count("sACT")

            for c in range(KD):
                b = 4 + c % 2
                w = [("sACT", kv_cast), ("sDVE", q_done_dve), bank_free[b]]
                mm(pb[b][0:64, :], kv_bf[0:64, c * 64:(c + 1) * 64],
                   q_g[0:64, c * TG:(c + 1) * TG], True, True, waits=w)
                mm(pb[b][64:128, :], kv_bf[64:128, c * 64:(c + 1) * 64],
                   q_g[64:128, c * TG:(c + 1) * TG], True, True,
                   tp=(64, 64), inc="sPE")
                pe_v = sch.count("sPE")
                d_ga = dve("tensor_tensor", q_g[:, c * TG:(c + 1) * TG],
                           pb[b][:, :], gt_g[:, c * TG:(c + 1) * TG],
                           OP.mult,
                           waits=[("sPE", pe_v), ("sACT", gate_act_end)])
                bank_free[b] = ("sDVE", d_ga)
            ga_dve = sch.count("sDVE")

            wo_slots = next_slots(4)
            for hh in range(4):
                load_half(wo_slots[hh], half_src(wo, hh))
            wo_w = slots_wait(wo_slots)
            w3p = [next_slots(2), next_slots(2)]
            w3_w = {}

            def load_w3(dc):
                sl = w3p[dc % 2]
                for h in range(2):
                    load_half(sl[h], half_src(w3t, dc * 2 + h))
                w3_w[dc] = slots_wait(sl)

            load_w3(0)
            load_w3(1)

            for dc in range(KD):
                b3 = 4 if dc % 2 == 0 else 6
                bo_b = 5 if dc % 2 == 0 else 7
                sl = w3p[dc % 2]
                for k3 in range(KH):
                    w = ()
                    if k3 == 0:
                        w = [bank_free[b3], ("sACT", g2_act_end)] + w3_w[dc]
                    mm(pb[b3][:, :],
                       pool[sl[k3 // 16]][:, (k3 % 16) * 128:
                                          (k3 % 16) * 128 + 128],
                       hb[:, k3 * TG:(k3 + 1) * TG],
                       k3 == 0, k3 == KH - 1, waits=w)
                for c in range(KD):
                    w = [("sDVE", ga_dve)]
                    if c == 0:
                        w += [bank_free[bo_b]] + wo_w
                    mm(pb[bo_b][:, :], sq_lhsT(wo_slots, c, dc),
                       q_g[:, c * TG:(c + 1) * TG], c == 0, c == KD - 1,
                       waits=w, inc="sPE" if c == KD - 1 else None)
                pe_v = sch.count("sPE")
                for s in sl:
                    slot_free[s] = ("sPE", pe_v)
                if dc + 2 < KD:
                    load_w3(dc + 2)
                bi = dc % 2
                dma("sync", xr[:, bi * 512:(bi + 1) * 512],
                    xT[dc * 128:(dc + 1) * 128, t0:t0 + TG],
                    waits=[prev_xr_read[bi]], inc=f"sXR{bi}")
                xr_v = (f"sXR{bi}", sch.count(f"sXR{bi}"))
                d_1a = dve("tensor_scalar", tA[:, :], pb[b3][:, :],
                           s3_c[:, dc:dc + 1], None, OP.mult,
                           waits=[("sPE", pe_v), tmp_free])
                d_1 = dve("tensor_tensor", tA[:, :], tA[:, :],
                          pb[bo_b][:, :], OP.add)
                bank_free[b3] = ("sDVE", d_1a)
                bank_free[bo_b] = ("sDVE", d_1)
                d_2 = dve("tensor_tensor", tB[:, :], tA[:, :],
                          xr[:, bi * 512:(bi + 1) * 512], OP.add,
                          waits=[xr_v, ("sACT", prev_ev_act)])
                tmp_free = ("sDVE", d_2)
                prev_xr_read[bi] = ("sDVE", d_2)
                ev = act(out_sb[:, bi * 512:(bi + 1) * 512], tB[:, :],
                         AF.Identity, bias=bo_c[:, dc:dc + 1],
                         waits=[("sDVE", d_2), prev_out_read[bi]])
                prev_ev_act = ev
                dma("sync", yT[dc * 128:(dc + 1) * 128, t0:t0 + TG],
                    out_sb[:, bi * 512:(bi + 1) * 512],
                    waits=[("sACT", ev)], inc=f"sST{bi}")
                store_n[0] += 1
                prev_out_read[bi] = (f"sST{bi}", sch.count(f"sST{bi}"))
            for s in wo_slots:
                slot_free[s] = ("sPE", sch.count("sPE"))

        sch.emit("sync", lambda e: e.nop(),
                 [("sST0", sch.count("sST0")),
                  ("sST1", sch.count("sST1"))], None)

        with nc.Block() as block:
            @block.tensor
            def _(e):
                sch.replay("tensor", e, sems)

            @block.vector
            def _(e):
                sch.replay("vector", e, sems)

            @block.scalar
            def _(e):
                sch.replay("scalar", e, sems)

            @block.gpsimd
            def _(e):
                sch.replay("gpsimd", e, sems)

            @block.sync
            def _(e):
                sch.replay("sync", e, sems)

    return nc


def _slabs(wT, nk):
    kin, nout = wT.shape
    nob = nout // (2048 // (kin // 128 // 2))
    kc = kin // 128
    kh = kc // 2
    osz = 2048 // kh
    nob_ = nout // osz
    a = wT.reshape(2, kh, 128, nob_, osz)
    a = a.transpose(3, 0, 2, 1, 4)
    return np.ascontiguousarray(a.reshape(nob_ * 2 * 128, 2048))


def kernel(**inputs):
    x = np.asarray(inputs["L0_attn_x"])
    base = {
        "wq": _slabs(np.asarray(inputs["L0_attn_Wq"]), None),
        "wk": _slabs(np.asarray(inputs["L0_attn_Wk"]), None),
        "wv": _slabs(np.asarray(inputs["L0_attn_Wv"]), None),
        "wg": _slabs(np.asarray(inputs["L0_attn_Wg"]), None),
        "wo": _slabs(np.asarray(inputs["L0_attn_Wo"]), None),
        "bq": np.asarray(inputs["L0_attn_bq"]),
        "bk": np.asarray(inputs["L0_attn_bk"]),
        "bv": np.asarray(inputs["L0_attn_bv"]),
        "bg": np.asarray(inputs["L0_attn_bg"]),
        "bo": np.asarray(inputs["L0_attn_bo"]),
        "w1t": _slabs(np.asarray(inputs["L0_ffn_w1_w"]).T, None),
        "w2t": _slabs(np.asarray(inputs["L0_ffn_w2_w"]).T, None),
        "h1t": _slabs(np.asarray(inputs["L0_ffn_h1_w"]).T, None),
        "h2t": _slabs(np.asarray(inputs["L0_ffn_h2_w"]).T, None),
        "w3t": _slabs(np.asarray(inputs["L0_ffn_w3_w"]).T, None),
        "s1": np.asarray(inputs["L0_ffn_w1_s"]),
        "s2": np.asarray(inputs["L0_ffn_w2_s"]),
        "sh1": np.asarray(inputs["L0_ffn_h1_s"]),
        "sh2": np.asarray(inputs["L0_ffn_h2_s"]),
        "s3": np.asarray(inputs["L0_ffn_w3_s"]),
    }
    in_maps = []
    for c in range(NCORES):
        b, t0 = c // 4, (c % 4) * T
        im = dict(base)
        im["xT"] = np.ascontiguousarray(x[b, t0:t0 + T, :].T)
        in_maps.append(im)
    nc = build_program()
    res = run_bass_kernel_spmd(nc, in_maps, core_ids=list(range(NCORES)))
    out = np.empty((2, 8192, 1024), np.float32)
    for c in range(NCORES):
        b, t0 = c // 4, (c % 4) * T
        out[b, t0:t0 + T, :] = res.results[c]["yT"].T
    return out
